# version 12 (final)
# baseline (speedup 1.0000x reference)
import numpy as np
import ml_dtypes
from contextlib import ExitStack

import concourse.bass as bass
import concourse.mybir as mybir
from concourse.bass_utils import run_bass_kernel_spmd

F32 = mybir.dt.float32
BF16 = mybir.dt.bfloat16
I32 = mybir.dt.int32
AF = mybir.ActivationFunctionType
ALU = mybir.AluOpType
AX = mybir.AxisListType

P = 128
T = 2048
D = 1024
H = 4
DK = 256
DV = 512
CH = 128
NCH = T // CH
GF = 2816
NT = T // P
NIF = D // P
NIC = GF // P
EPS = 1e-5
MAGIC = 12582912.0
MAGIC_H = 6291456.0
INV6 = float(np.float32(1.0 / 6.0))
SQ2 = float(np.sqrt(np.float64(2.0)))
C63 = 2.0 ** 63.5
SEL_A = (SQ2 - 1.0) / 0.5 * C63
SEL_B = (1.0 - (SQ2 - 1.0) / 0.5) * C63

ENGINES = ("sync", "vector", "scalar", "gpsimd", "tensor")


def host_constants():
    c = {}
    hs = np.arange(H)
    gamma = (1.0 - np.exp2(-5.0 - hs)).astype(np.float64)
    i = np.arange(CH, dtype=np.float64)
    dm = np.zeros((H, CH, CH), np.float64)
    for h in range(H):
        e = i[None, :] - i[:, None]
        dm[h] = np.where(e >= 0, gamma[h] ** np.maximum(e, 0.0), 0.0) / 16.0
    c["dmaskT"] = dm.astype(ml_dtypes.bfloat16)
    cr = np.stack([gamma[h] ** (i + 1.0) / 16.0 for h in range(H)])
    c["crossb"] = np.broadcast_to(cr[:, None, :], (H, P, CH)).astype(ml_dtypes.bfloat16).copy()
    kd = np.stack([gamma[h] ** (CH - 1.0 - i) for h in range(H)], axis=1)
    c["kdec"] = kd.astype(np.float32)
    c["gC"] = [float(g ** CH) for g in gamma]
    inv_freq = 1.0 / (10000.0 ** (np.arange(0, DK, 2, dtype=np.float64) / DK))
    freqs = np.arange(T, dtype=np.float64)[None, :] * inv_freq[:, None]
    c["cosf"] = np.cos(freqs).astype(np.float32)
    c["sinf"] = np.sin(freqs).astype(np.float32)
    c["ident"] = np.eye(P, dtype=ml_dtypes.bfloat16)
    c["ones1"] = np.ones((1, P), dtype=ml_dtypes.bfloat16)
    return c


class Sem:
    def __init__(self, handle):
        self.h = handle
        self.n = 0


class Prog:

    def __init__(self, nc, es):
        self.nc = nc
        self.ops = {e: [] for e in ENGINES}
        self.sems = {}
        for name in ("mv", "ma", "mg", "mp", "qs", "qa", "qg"):
            self.sems[name] = Sem(es.enter_context(nc.semaphore("sem_" + name)))
        self.bankuse = {}
        self.dbg_on = False
        self.dbg_seen = set()

    def dbg(self, name, ap, waits=()):
        if not self.dbg_on or name in self.dbg_seen:
            return
        self.dbg_seen.add(name)
        sh = list(ap.shape)
        dt_ = ap.dtype
        t = self.nc.dram_tensor("dbg_" + name, sh, dt_, kind="ExternalOutput").ap()
        self.dma("gpsimd", lambda e, t=t, ap=ap: e.dma_start(out=t, in_=ap),
                 waits=list(waits))
        self.barrier()

    def mark_of(self, eng):
        return {"vector": self.sems["mv"], "scalar": self.sems["ma"],
                "gpsimd": self.sems["mg"], "tensor": self.sems["mp"],
                "sync": self.sems["qs"]}[eng]

    def op(self, eng, fn, waits=(), mark=False, edge=False):
        ms = self.mark_of(eng)
        waits = [w for w in waits if w is not None]
        incs = [(ms, 1)] if (mark or edge) else []
        for s, k in incs:
            s.n += k
        self.ops[eng].append((fn, waits, [(s, k) for s, k in incs]))
        res = (ms, ms.n) if (mark or edge) else None
        if edge:
            self.ops[eng].append((None, [(ms, ms.n)], []))
        return res

    def dma(self, queue, fn, waits=()):
        qs = {"sync": self.sems["qs"], "scalar": self.sems["qa"],
              "gpsimd": self.sems["qg"]}[queue]
        waits = [w for w in waits if w is not None]
        qs.n += 16
        self.ops[queue].append((fn, waits, [(qs, 16)]))
        return (qs, qs.n)

    def barrier(self):
        snap = [(s, s.n) for s in self.sems.values() if s.n > 0]
        for e in ENGINES:
            self.ops[e].append((None, snap, []))

    def use_bank(self, name, consumer_mark):
        self.bankuse[name] = consumer_mark

    def bank_wait(self, name):
        return self.bankuse.get(name)

    def build(self, block):
        def runner(eng_name):
            def _(eng):
                for fn, waits, incs in self.ops[eng_name]:
                    for s, val in waits:
                        eng.wait_ge(s.h, val)
                    if fn is None:
                        continue
                    inst = fn(eng)
                    for s, k in incs:
                        inst.then_inc(s.h, k)
            return _
        block.sync(runner("sync"))
        block.vector(runner("vector"))
        block.scalar(runner("scalar"))
        block.gpsimd(runner("gpsimd"))
        block.tensor(runner("tensor"))


def build_program(dev_dump=False, dbg=False):
    nc = bass.Bass("TRN2", target_bir_lowering=False, debug=False)
    CONST = host_constants()
    GAMMA_C = CONST["gC"]

    x_in = nc.dram_tensor("x0", [T, D], F32, kind="ExternalInput").ap()
    w_q = nc.dram_tensor("w_q", [D, D], F32, kind="ExternalInput").ap()
    w_k = nc.dram_tensor("w_k", [D, D], F32, kind="ExternalInput").ap()
    w_v = nc.dram_tensor("w_v", [D, 2 * D], F32, kind="ExternalInput").ap()
    w_g = nc.dram_tensor("w_g", [D, 2 * D], F32, kind="ExternalInput").ap()
    w_o = nc.dram_tensor("w_o", [2 * D, D], F32, kind="ExternalInput").ap()
    w_gate = nc.dram_tensor("w_gate", [GF, D], F32, kind="ExternalInput").ap()
    w_up = nc.dram_tensor("w_up", [GF, D], F32, kind="ExternalInput").ap()
    w_down = nc.dram_tensor("w_down", [D, GF], F32, kind="ExternalInput").ap()
    lnw_c = nc.dram_tensor("lnw_c", [P, NIF], F32, kind="ExternalInput").ap()
    lnb_c = nc.dram_tensor("lnb_c", [P, NIF], F32, kind="ExternalInput").ap()
    gnw_b = nc.dram_tensor("gnw_b", [P, DV], F32, kind="ExternalInput").ap()
    dmaskT = nc.dram_tensor("dmaskT", [H, CH, CH], BF16, kind="ExternalInput").ap()
    crossb = nc.dram_tensor("crossb", [H, P, CH], BF16, kind="ExternalInput").ap()
    kdec_in = nc.dram_tensor("kdec", [P, H], F32, kind="ExternalInput").ap()
    cosf = nc.dram_tensor("cosf", [P, T], F32, kind="ExternalInput").ap()
    sinf = nc.dram_tensor("sinf", [P, T], F32, kind="ExternalInput").ap()
    ident_in = nc.dram_tensor("ident", [P, P], BF16, kind="ExternalInput").ap()
    ones1_in = nc.dram_tensor("ones1", [1, P], BF16, kind="ExternalInput").ap()

    out_t = nc.dram_tensor("out", [T, D], F32, kind="ExternalOutput").ap()
    dbg_ofin_all = nc.dram_tensor("dbg_ofin_all", [T, 2 * DV], BF16,
                                  kind="ExternalOutput").ap()
    dbg_ofm2_all = nc.dram_tensor("dbg_ofm2_all", [T, 2 * DV], BF16,
                                  kind="ExternalOutput").ap()
    dbg_wops_all = nc.dram_tensor("dbg_wops_all", [T, 2 * DV], F32,
                                  kind="ExternalOutput").ap()
    dbg_ofm_all = nc.dram_tensor("dbg_ofm_all", [T, 2 * DV], BF16,
                                 kind="ExternalOutput").ap()
    dbg_os_all = nc.dram_tensor("dbg_os_all", [2 * NCH, P, 4], F32,
                                kind="ExternalOutput").ap()
    skind = "ExternalOutput" if dev_dump else "Internal"
    x1_s = nc.dram_tensor("x1_s", [T, D], F32, kind=skind).ap()
    x2p_s = nc.dram_tensor("x2p_s", [T, D], F32, kind=skind).ap()
    x2_s = nc.dram_tensor("x2_s", [T, D], F32, kind=skind).ap()
    hq_s = nc.dram_tensor("hq_s", [T, GF], BF16, kind="Internal").ap()
    wgq_s = nc.dram_tensor("wgq_s", [D, GF], BF16, kind="Internal").ap()
    wuq_s = nc.dram_tensor("wuq_s", [D, GF], BF16, kind="Internal").ap()
    wdq_s = nc.dram_tensor("wdq_s", [D, GF], BF16, kind="Internal").ap()

    with ExitStack() as es:
        def sb(name, shape, dtype):
            return es.enter_context(nc.sbuf_tensor(name, shape, dtype))

        wbf = sb("wbf", [P, 45056], BF16)
        zq_fm = sb("zq_fm", [P, 2 * D], BF16)
        cs_sb = sb("cs_sb", [P, 4 * CH], F32)
        dm_sb = sb("dm_sb", [P, H * CH], BF16)
        cr_sb = sb("cr_sb", [P, H * CH], BF16)
        kd_sb = sb("kd_sb", [P, H], F32)
        gnw_sb = sb("gnw_sb", [P, DV], F32)
        lnw_sb = sb("lnw_sb", [P, NIF], F32)
        lnb_sb = sb("lnb_sb", [P, NIF], F32)
        lnbbf_sb = sb("lnbbf_sb", [P, NIF], BF16)
        id_sb = sb("id_sb", [P, P], BF16)
        ones_sb = sb("ones_sb", [1, P], BF16)
        PADW = 160
        st_sum = sb("st_sum", [P, PADW], F32)
        st_sq = sb("st_sq", [P, PADW], F32)
        st_nmu = sb("st_nmu", [P, PADW], F32)
        st_var = sb("st_var", [P, PADW], F32)
        st_args = sb("st_args", [P, PADW], F32)
        st_arg2 = sb("st_arg2", [P, PADW], F32)
        st_rstd1 = sb("st_rstd1", [P, PADW], F32)
        st_R = sb("st_R", [P, PADW], F32)
        st_t1 = sb("st_t1", [P, PADW], F32)
        st_t2 = sb("st_t2", [P, PADW], F32)
        st_t3 = sb("st_t3", [P, PADW], F32)
        S_f = sb("S_f", [P, 2 * 2 * DV], F32)
        S_b = sb("S_b", [P, 2 * 2 * DV], BF16)
        bias_sb = sb("bias_sb", [P, 8], F32)
        bvrow = sb("bvrow", [1, 3 * D], BF16)
        osum = st_arg2
        orst = st_sum
        ot1 = st_t1
        ot2 = st_t2
        ot3 = st_t3
        ot4 = st_args

        QW = 1408
        xa = sb("xa", [P, D], F32)
        xb = sb("xb", [P, D], F32)
        hf = sb("hf", [P, GF], F32)
        hf2 = hf
        q_az = sb("q_az", [P, QW], F32)
        q_sg = sb("q_sg", [P, QW], F32)
        q_u = sb("q_u", [P, QW], F32)
        q_h1 = sb("q_h1", [P, QW], F32)
        q_h2 = sb("q_h2", [P, QW], F32)
        q_a3 = sb("q_a3", [P, QW], F32)
        q_bm = sb("q_bm", [P, QW // 16], F32)
        q_s = sb("q_s", [P, QW // 16], F32)
        q_rs = sb("q_rs", [P, QW // 16], F32)
        sil = sb("sil", [P, 512], F32)
        zq_t = sb("zq_t", [P, D], BF16)
        hq_t = sb("hq_t", [P, GF], BF16)
        hq_t2 = sb("hq_t2", [P, GF], BF16)
        wstage = hf
        wstage2 = hf2
        hqf = q_az[:].bitcast(BF16)
        hqf2 = q_sg[:].bitcast(BF16)
        wraw = sb("wraw", [P, 1024], BF16)
        n2a = sb("n2a", [P, D], F32)
        zt = n2a
        x2pc = n2a
        nrm_bf = q_h1[:].bitcast(BF16)[:, 0:D]
        r_tt = q_u[:, 0:8 * CH]
        qbf = q_h2[:].bitcast(BF16)[:, 0:512]
        kbf = q_h2[:].bitcast(BF16)[:, 512:1024]
        ktm = q_h2[:].bitcast(BF16)[:, 1024:1536]
        vsb = q_h2[:].bitcast(BF16)[:, 1536:2560]
        vnp = q_a3[:].bitcast(BF16)[:, 0:512]
        tgt = q_a3[:].bitcast(BF16)[:, 512:1536]
        scm = q_a3[:].bitcast(BF16)[:, 1536:1664]
        qc1 = q_a3[:].bitcast(BF16)[:, 1664:1792]
        qc2 = q_a3[:].bitcast(BF16)[:, 1792:1920]
        ofin = q_az[:].bitcast(BF16)[:, 0:1024]
        ofm = q_az[:].bitcast(BF16)[:, 1024:2048]
        nbf = q_sg[:].bitcast(BF16)
        sqsc = q_h1[:, QW - 512:QW]

        def psn(name, shape, dtype):
            return es.enter_context(nc.psum_tensor(name, shape, dtype))
        B_tr = psn("B_tr", [P, 1024], BF16)
        B_a = psn("B_a", [P, 512], F32)
        B_b = psn("B_b", [P, 512], F32)
        B_c = psn("B_c", [P, 512], F32)
        B_d = psn("B_d", [P, 512], F32)
        B_e = psn("B_e", [P, 512], F32)
        B_f = psn("B_f", [P, 512], F32)
        B_g = psn("B_g", [P, 512], F32)

        p = Prog(nc, es)
        p.dbg_on = dbg

        cl = {}
        cl["dm"] = p.dma("sync", lambda e: e.dma_start(
            out=dm_sb[:].rearrange("p (h c) -> p h c", h=H),
            in_=dmaskT.rearrange("h j c -> j h c")))
        cl["cr"] = p.dma("sync", lambda e: e.dma_start(
            out=cr_sb[:].rearrange("p (h c) -> p h c", h=H),
            in_=crossb.rearrange("h p c -> p h c")))
        cl["kd"] = p.dma("sync", lambda e: e.dma_start(out=kd_sb[:], in_=kdec_in))
        cl["gnw"] = p.dma("sync", lambda e: e.dma_start(out=gnw_sb[:], in_=gnw_b))
        cl["lnw"] = p.dma("sync", lambda e: e.dma_start(out=lnw_sb[:], in_=lnw_c))
        cl["lnb"] = p.dma("sync", lambda e: e.dma_start(out=lnb_sb[:], in_=lnb_c))
        cl["id"] = p.dma("sync", lambda e: e.dma_start(out=id_sb[:], in_=ident_in))
        cl["ones"] = p.dma("sync", lambda e: e.dma_start(out=ones_sb[:],
                                                         in_=ones1_in))
        mc = cl["ones"]
        m_lnbbf = p.op("vector", lambda v: v.tensor_copy(out=lnbbf_sb[:], in_=lnb_sb[:]),
                       waits=[cl["lnb"]], mark=True)

        def emit_rsqrt(out, vin, t1, t2, t3, iters=3):
            E = lambda fn: p.op("vector", fn)
            E(lambda v: v.tensor_scalar(out=t1.bitcast(I32), in0=vin.bitcast(I32),
                                        scalar1=0x7F800000, scalar2=None,
                                        op0=ALU.bitwise_and))
            E(lambda v: v.reciprocal(out=t2, in_=t1))
            E(lambda v: v.tensor_tensor(out=t1, in0=vin, in1=t2, op=ALU.mult))
            E(lambda v: v.tensor_scalar(out=t2.bitcast(I32), in0=t2.bitcast(I32),
                                        scalar1=1, scalar2=None,
                                        op0=ALU.logical_shift_right))
            E(lambda v: v.tensor_scalar(out=t3.bitcast(I32), in0=t2.bitcast(I32),
                                        scalar1=0x00400000, scalar2=0x3F800000,
                                        op0=ALU.bitwise_and, op1=ALU.bitwise_or))
            E(lambda v: v.tensor_scalar(out=t3, in0=t3, scalar1=SEL_A, scalar2=SEL_B,
                                        op0=ALU.mult, op1=ALU.add))
            E(lambda v: v.tensor_scalar(out=t2.bitcast(I32), in0=t2.bitcast(I32),
                                        scalar1=0x7F800000, scalar2=None,
                                        op0=ALU.bitwise_and))
            E(lambda v: v.tensor_tensor(out=t2, in0=t2, in1=t3, op=ALU.mult))
            E(lambda v: v.tensor_scalar(out=out, in0=t1, scalar1=-0.22416,
                                        scalar2=1.22491, op0=ALU.mult, op1=ALU.add))
            for _ in range(iters):
                E(lambda v: v.tensor_tensor(out=t3, in0=out, in1=out, op=ALU.mult))
                E(lambda v: v.tensor_tensor(out=t3, in0=t3, in1=t1, op=ALU.mult))
                E(lambda v: v.tensor_scalar(out=t3, in0=t3, scalar1=-0.5, scalar2=1.5,
                                            op0=ALU.mult, op1=ALU.add))
                E(lambda v: v.tensor_tensor(out=out, in0=out, in1=t3, op=ALU.mult))
            return p.op("vector", lambda v: v.tensor_tensor(out=out, in0=out,
                                                            in1=t2, op=ALU.mult),
                        mark=True)

        def emit_quant(src_ap, dst_bf_ap, ncols, dep_waits, final_scale=None):
            if ncols > QW:
                m = None
                nch_ = (ncols + QW - 1) // QW
                step = ncols // nch_
                assert step % 16 == 0 and ncols % nch_ == 0
                for j in range(nch_):
                    m = emit_quant(src_ap[:, j * step:(j + 1) * step],
                                   dst_bf_ap[:, j * step:(j + 1) * step],
                                   step, dep_waits, final_scale)
                return m
            nb = ncols // 16
            az, sg = q_az[:, 0:ncols], q_sg[:, 0:ncols]
            u, h1 = q_u[:, 0:ncols], q_h1[:, 0:ncols]
            h2, a3 = q_h2[:, 0:ncols], q_a3[:, 0:ncols]
            bm, ss, rs = q_bm[:, 0:nb], q_s[:, 0:nb], q_rs[:, 0:nb]
            src3 = src_ap.rearrange("p (a b) -> p a b", b=16)
            war = [(p.sems["mv"], p.sems["mv"].n), (p.sems["mg"], p.sems["mg"].n)]
            dep = list(dep_waits) + war
            m_az = p.op("scalar", lambda s: s.activation(out=az, in_=src_ap,
                                                         func=AF.Abs),
                        waits=dep, mark=True)
            m_sg = p.op("scalar", lambda s: s.activation(out=sg, in_=src_ap,
                                                         func=AF.Sign), mark=True)
            m_bm = p.op("vector", lambda v: v.reduce_max(
                out=bm.rearrange("p (a b) -> p a b", b=1), in_=src3, axis=AX.X,
                apply_absolute_value=True), waits=list(dep_waits), mark=True)
            p.op("vector", lambda v: v.tensor_scalar(
                out=u[:, 0:256], in0=u[:, 0:256], scalar1=1.0, scalar2=None,
                op0=ALU.mult), waits=[m_bm])
            p.op("vector", lambda v: v.tensor_scalar(
                out=ss, in0=bm, scalar1=INV6, scalar2=1e-12,
                op0=ALU.mult, op1=ALU.max))
            p.op("vector", lambda v: v.tensor_scalar(
                out=u[:, 0:256], in0=u[:, 0:256], scalar1=1.0, scalar2=None,
                op0=ALU.mult))
            m_rs = p.op("vector", lambda v: v.reciprocal(out=rs, in_=ss),
                        mark=True)
            p.op("vector", lambda v: v.tensor_scalar(
                out=u[:, 0:256], in0=u[:, 0:256], scalar1=1.0, scalar2=None,
                op0=ALU.mult))
            if final_scale is not None:
                p.op("vector", lambda v: v.tensor_scalar(
                    out=ss, in0=ss, scalar1=final_scale, scalar2=None,
                    op0=ALU.mult))
                p.op("vector", lambda v: v.tensor_scalar(
                    out=u[:, 0:256], in0=u[:, 0:256], scalar1=1.0, scalar2=None,
                    op0=ALU.mult))
            rs_b = rs.rearrange("p (a b) -> p a b", b=1).broadcast_to([P, nb, 16])
            s_b = ss.rearrange("p (a b) -> p a b", b=1).broadcast_to([P, nb, 16])
            m_u = p.op("vector", lambda v: v.tensor_tensor(
                out=u.rearrange("p (a b) -> p a b", b=16),
                in0=az.rearrange("p (a b) -> p a b", b=16), in1=rs_b, op=ALU.mult),
                waits=[m_az, m_rs], mark=True)
            m_h1 = p.op("gpsimd", lambda g: g.tensor_scalar(
                out=h1, in0=u, scalar1=MAGIC_H, scalar2=MAGIC_H,
                op0=ALU.add, op1=ALU.subtract), waits=[m_u], mark=True)
            p.op("vector", lambda v: v.tensor_scalar(
                out=h2, in0=u, scalar1=MAGIC - 2.0, scalar2=MAGIC,
                op0=ALU.add, op1=ALU.subtract), waits=[m_u])
            p.op("vector", lambda v: v.tensor_scalar(
                out=h2, in0=h2, scalar1=0.0, scalar2=2.0,
                op0=ALU.max, op1=ALU.min))
            p.op("vector", lambda v: v.tensor_scalar(
                out=a3, in0=u, scalar1=5.0, scalar2=2.0,
                op0=ALU.is_gt, op1=ALU.mult))
            p.op("vector", lambda v: v.scalar_tensor_tensor(
                out=h1, in0=h1, scalar=2.0, in1=h2, op0=ALU.min, op1=ALU.add),
                waits=[m_h1])
            p.op("vector", lambda v: v.tensor_tensor(out=h1, in0=h1, in1=a3,
                                                     op=ALU.add))
            p.op("vector", lambda v: v.tensor_tensor(out=h1, in0=h1, in1=sg,
                                                     op=ALU.mult), waits=[m_sg])
            dst_war = [(p.sems["mp"], p.sems["mp"].n),
                       (p.sems["qa"], p.sems["qa"].n)]
            return p.op("vector", lambda v: v.tensor_tensor(
                out=dst_bf_ap.rearrange("p (a b) -> p a b", b=16),
                in0=h1.rearrange("p (a b) -> p a b", b=16), in1=s_b, op=ALU.mult),
                waits=dst_war, mark=True)

        def emit_stats_pass(x_src, with_R):
            m1 = m2 = None
            for t in range(NT):
                buf = xa if t % 2 == 0 else xb
                mld = p.dma("sync", lambda e, b=buf, t=t: e.dma_start(
                    out=b[:], in_=x_src[t * P:(t + 1) * P, :]),
                    waits=[(p.sems["ma"], p.sems["ma"].n),
                           (p.sems["mv"], p.sems["mv"].n)])
                m1 = p.op("scalar", lambda s, b=buf, t=t: s.activation(
                    out=q_az[:, 0:D], in_=b[:], func=AF.Identity,
                    accum_out=st_sum[:, t:t + 1]), waits=[mld], mark=True)
                m2 = p.op("scalar", lambda s, b=buf, t=t: s.activation(
                    out=q_sg[:, 0:D], in_=b[:], func=AF.Square,
                    accum_out=st_sq[:, t:t + 1]), waits=[mld], mark=True)
            E = lambda fn, w=(): p.op("vector", fn, waits=w)
            E(lambda v: v.tensor_scalar(out=q_u[:, 0:256], in0=q_u[:, 0:256],
                                        scalar1=1.0, scalar2=None,
                                        op0=ALU.mult), w=[m1, m2])
            E(lambda v: v.tensor_scalar(out=q_u[:, 256:512], in0=q_u[:, 256:512],
                                        scalar1=1.0, scalar2=None, op0=ALU.mult))
            E(lambda v: v.tensor_scalar(out=st_nmu[:], in0=st_sum[:],
                                        scalar1=-1.0 / D, scalar2=None,
                                        op0=ALU.mult))
            E(lambda v: v.tensor_tensor(out=st_t1[:], in0=st_nmu[:],
                                        in1=st_nmu[:], op=ALU.mult))
            E(lambda v: v.scalar_tensor_tensor(out=st_var[:], in0=st_sq[:],
                                               scalar=1.0 / D, in1=st_t1[:],
                                               op0=ALU.mult, op1=ALU.subtract))
            E(lambda v: v.tensor_scalar(out=st_args[:], in0=st_var[:],
                                        scalar1=EPS, scalar2=None, op0=ALU.add))
            if with_R:
                E(lambda v: v.reciprocal(out=st_t1[:], in_=st_args[:]))
                E(lambda v: v.tensor_tensor(out=st_t2[:], in0=st_var[:],
                                            in1=st_t1[:], op=ALU.mult))
                E(lambda v: v.tensor_scalar(out=st_t2[:], in0=st_t2[:],
                                            scalar1=EPS, scalar2=None,
                                            op0=ALU.add))
                E(lambda v: v.tensor_tensor(out=st_arg2[:], in0=st_args[:],
                                            in1=st_t2[:], op=ALU.mult))
            m = emit_rsqrt(st_rstd1[:], st_args[:], st_t1[:], st_t2[:], st_t3[:])
            if with_R:
                m = emit_rsqrt(st_R[:], st_arg2[:], st_t1[:], st_t2[:], st_t3[:])
            return m

        def f_phase(x_src, x_dst, first):
            wgq = [wbf[:, c * GF:(c + 1) * GF] for c in range(NIF)]
            wuq = [wbf[:, 22528 + c * GF:22528 + (c + 1) * GF] for c in range(NIF)]
            wdq = [wbf[:, c * D:(c + 1) * D] for c in range(NIC)]

            m_stats = emit_stats_pass(x_src, with_R=False)

            w_marks = {}
            if first:
                for wi, (wsrc, wdst, spill, base) in enumerate(
                        ((w_gate, wgq, wgq_s, 0), (w_up, wuq, wuq_s, 22528))):
                    for r in range(NIC):
                        stg = wstage if r % 2 == 0 else wstage2
                        mld = p.dma("sync", lambda e, s_=stg, r=r, w=wsrc:
                                    e.dma_start(out=s_[:, 0:D],
                                                in_=w[r * P:(r + 1) * P, :]),
                                    waits=[(p.sems["ma"], p.sems["ma"].n),
                                           (p.sems["mv"], p.sems["mv"].n),
                                           (p.sems["mg"], p.sems["mg"].n)])
                        mq = emit_quant(stg[:, 0:D], zq_t[:], D, [mld])
                        mtr = None
                        for c_ in range(NIF):
                            mtr = p.op("tensor", lambda tn, c_=c_: tn.transpose(
                                out=B_tr[:, c_ * P:(c_ + 1) * P],
                                in_=zq_t[:, c_ * P:(c_ + 1) * P], identity=id_sb[:]),
                                waits=[mq, mc, p.bank_wait("tr")] if c_ == 0
                                else [], mark=(c_ == NIF - 1))
                        dst = wbf[:, base:base + NIF * GF].rearrange(
                            "p (c g) -> p c g", c=NIF)[:, :, r * P:(r + 1) * P]
                        mv = p.op("vector", lambda v, dst=dst: v.tensor_copy(
                            out=dst,
                            in_=B_tr[:, 0:1024].rearrange("p (c g) -> p c g",
                                                          c=NIF)),
                            waits=[mtr], mark=True)
                        p.use_bank("tr", mv)
                        w_marks[(wi, r)] = mv
                    for c_ in range(NIF):
                        p.dma("scalar", lambda e, c_=c_, wd=wdst, sp=spill:
                              e.dma_start(out=sp[c_ * P:(c_ + 1) * P, :],
                                          in_=wd[c_][:]),
                              waits=[w_marks[(wi, NIC - 1)]])
                m_wgu = w_marks[(1, NIC - 1)]
                p.dbg("wgq_c0", wgq[0][:], [m_wgu])
                p.dbg("wuq_c0", wuq[0][:], [m_wgu])
                for r in range(NIF):
                    stg = wstage if r % 2 == 0 else wstage2
                    mld = p.dma("sync", lambda e, s_=stg, r=r: e.dma_start(
                        out=s_[:], in_=w_down[r * P:(r + 1) * P, :]),
                        waits=[(p.sems["ma"], p.sems["ma"].n),
                               (p.sems["mv"], p.sems["mv"].n),
                               (p.sems["mg"], p.sems["mg"].n)])
                    mq = emit_quant(stg[:], hq_t[:], GF, [mld], final_scale=0.5)
                    p.dma("scalar", lambda e, r=r: e.dma_start(
                        out=wdq_s[r * P:(r + 1) * P, :], in_=hq_t[:]),
                        waits=[mq])
                m_wd_spill = (p.sems["qa"], p.sems["qa"].n)
            else:
                last = None
                for c_ in range(NIF):
                    last = p.dma("sync", lambda e, c_=c_: e.dma_start(
                        out=wgq[c_][:], in_=wgq_s[c_ * P:(c_ + 1) * P, :]))
                    last = p.dma("sync", lambda e, c_=c_: e.dma_start(
                        out=wuq[c_][:], in_=wuq_s[c_ * P:(c_ + 1) * P, :]))
                m_wgu = last
                m_wd_spill = None

            GFC = ((0, 512), (512, 512), (1024, 512), (1536, 512), (2048, 512),
                   (2560, 256))
            hq_m = {}
            zq_m = {}
            for t in range(NT):
                buf = xa if t % 2 == 0 else xb
                zsl = (t % 2) * D
                mld = p.dma("sync", lambda e, b=buf, t=t: e.dma_start(
                    out=b[:], in_=x_src[t * P:(t + 1) * P, :]),
                    waits=[(p.sems["ma"], p.sems["ma"].n),
                           (p.sems["mv"], p.sems["mv"].n)])
                mz = p.op("vector", lambda v, b=buf, t=t: v.tensor_scalar(
                    out=zt[:], in0=b[:], scalar1=st_nmu[:, t:t + 1],
                    scalar2=st_rstd1[:, t:t + 1], op0=ALU.add, op1=ALU.mult),
                    waits=[mld, m_stats], mark=True)
                mq = emit_quant(zt[:], zq_t[:], D, [mz])
                if t == 0:
                    p.dbg("stats_nmu", st_nmu[:], [m_stats])
                    p.dbg("stats_rstd", st_rstd1[:, 0:NT], [m_stats])
                    p.dbg("z0", zt[:], [mz, mq])
                    p.dbg("zq0", zq_t[:], [mq])
                mtr = None
                for c_ in range(NIF):
                    mtr = p.op("tensor", lambda tn, c_=c_: tn.transpose(
                        out=B_tr[:, c_ * P:(c_ + 1) * P],
                        in_=zq_t[:, c_ * P:(c_ + 1) * P], identity=id_sb[:]),
                        waits=[mq, mc, p.bank_wait("tr")] if c_ == 0 else [],
                        mark=(c_ == NIF - 1))
                zq_m[t] = p.op("vector", lambda v, zsl=zsl: v.tensor_copy(
                    out=zq_fm[:, zsl:zsl + D], in_=B_tr[:, 0:1024]),
                    waits=[mtr, (p.sems["mp"], p.sems["mp"].n)], mark=True)
                p.use_bank("tr", zq_m[t])
                if t == 0:
                    p.dbg("zqfm0", zq_fm[:, 0:D], [zq_m[t]])
                hbuf = hf if t % 2 == 0 else hf2
                hqb = hq_t if t % 2 == 0 else hq_t2
                pg = B_a if t % 2 == 0 else B_c
                pu = B_b if t % 2 == 0 else B_d
                bg = "ga" if t % 2 == 0 else "gc"
                bu = "gb" if t % 2 == 0 else "gd"
                mh = None
                for gi, (g0, gn) in enumerate(GFC):
                    mgate = None
                    for c_ in range(NIF):
                        w0 = []
                        if c_ == 0:
                            w0 = [zq_m[t], m_wgu, p.bank_wait(bg)]
                        mgate = p.op("tensor", lambda tn, c_=c_, g0=g0, gn=gn, t=t,
                                     pg=pg: tn.matmul(
                            out=pg[:, 0:gn],
                            lhsT=zq_fm[:, (t % 2) * D + c_ * P:
                                        (t % 2) * D + (c_ + 1) * P],
                            rhs=wgq[c_][:, g0:g0 + gn],
                            start=(c_ == 0), stop=(c_ == NIF - 1)),
                            waits=w0, mark=(c_ == NIF - 1))
                    mup = None
                    for c_ in range(NIF):
                        w0 = [p.bank_wait(bu)] if c_ == 0 else []
                        mup = p.op("tensor", lambda tn, c_=c_, g0=g0, gn=gn, t=t,
                                   pu=pu: tn.matmul(
                            out=pu[:, 0:gn],
                            lhsT=zq_fm[:, (t % 2) * D + c_ * P:
                                        (t % 2) * D + (c_ + 1) * P],
                            rhs=wuq[c_][:, g0:g0 + gn],
                            start=(c_ == 0), stop=(c_ == NIF - 1)),
                            waits=w0, mark=(c_ == NIF - 1))
                    msil = p.op("scalar", lambda s, gn=gn, pg=pg:
                                s.activation(out=sil[:, 0:gn],
                                             in_=pg[:, 0:gn], func=AF.Silu),
                                waits=[mgate, (p.sems["mv"], p.sems["mv"].n)],
                                mark=True)
                    p.use_bank(bg, msil)
                    mh = p.op("vector", lambda v, g0=g0, gn=gn, hb=hbuf, pu=pu:
                              v.tensor_tensor(out=hb[:, g0:g0 + gn],
                                              in0=sil[:, 0:gn],
                                              in1=pu[:, 0:gn], op=ALU.mult),
                              waits=[msil, mup], mark=True)
                    p.use_bank(bu, mh)
                mq = emit_quant(hbuf[:], hqb[:], GF, [mh])
                if t == 0:
                    p.dbg("h0", hbuf[:], [mh, mq])
                    p.dbg("hq0", hqb[:], [mq])
                hq_m[t] = p.dma("scalar", lambda e, t=t, hqb=hqb: e.dma_start(
                    out=hq_s[t * P:(t + 1) * P, :], in_=hqb[:]), waits=[mq])

            mwd = None
            for r in range(NIF):
                stg = wstage if r % 2 == 0 else wstage2
                w0 = [m_wd_spill] if r == 0 else []
                w0 = w0 + [(p.sems["mp"], p.sems["mp"].n),
                           (p.sems["ma"], p.sems["ma"].n),
                           (p.sems["mv"], p.sems["mv"].n),
                           (p.sems["mg"], p.sems["mg"].n)]
                mldw = p.dma("sync", lambda e, s_=stg, r=r: e.dma_start(
                    out=s_[:].bitcast(BF16)[:, 0:GF],
                    in_=wdq_s[r * P:(r + 1) * P, :]), waits=w0)
                for c0 in range(0, NIC, 8):
                    cn = min(8, NIC - c0)
                    mtr = None
                    for j in range(cn):
                        mtr = p.op("tensor", lambda tn, c_=c0 + j, j=j, s_=stg:
                                   tn.transpose(
                                       out=B_tr[:, j * P:(j + 1) * P],
                                       in_=s_[:].bitcast(BF16)[:, c_ * P:
                                                               (c_ + 1) * P],
                                       identity=id_sb[:]),
                                   waits=[mldw, p.bank_wait("tr")] if j == 0
                                   else [], mark=(j == cn - 1))
                    dst = wbf[:, 0:NIC * D].rearrange(
                        "p (c g) -> p c g", c=NIC)[:, c0:c0 + cn,
                                                   r * P:(r + 1) * P]
                    mwd = p.op("vector", lambda v, dst=dst, cn=cn: v.tensor_copy(
                        out=dst, in_=B_tr[:, 0:cn * P].rearrange(
                            "p (c g) -> p c g", c=cn)),
                        waits=[mtr], mark=True)
                    p.use_bank("tr", mwd)
            p.dbg("wdq_c0", wdq[0][:], [mwd])
            for t in range(NT):
                hqb = hq_t if t % 2 == 0 else hq_t2
                hqfm = hqf if t % 2 == 0 else hqf2
                x0b = xa if t % 2 == 0 else xb
                mlh = p.dma("sync", lambda e, t=t, hqb=hqb: e.dma_start(
                    out=hqb[:], in_=hq_s[t * P:(t + 1) * P, :]),
                    waits=[hq_m[t], (p.sems["mp"], p.sems["mp"].n)])
                mlx = p.dma("sync", lambda e, t=t, b=x0b: e.dma_start(
                    out=b[:], in_=x_src[t * P:(t + 1) * P, :]),
                    waits=[(p.sems["qa"], p.sems["qa"].n)])
                mevs = []
                for c0 in range(0, NIC, 8):
                    cn = min(8, NIC - c0)
                    mtr = None
                    for j in range(cn):
                        mtr = p.op("tensor", lambda tn, c_=c0 + j, j=j, hqb=hqb:
                                   tn.transpose(out=B_tr[:, j * P:(j + 1) * P],
                                                in_=hqb[:, c_ * P:(c_ + 1) * P],
                                                identity=id_sb[:]),
                                   waits=[mlh, mwd, p.bank_wait("tr")] if j == 0
                                   else [], mark=(j == cn - 1))
                    mv = p.op("vector", lambda v, c0=c0, cn=cn, hqfm=hqfm:
                              v.tensor_copy(out=hqfm[:, c0 * P:(c0 + cn) * P],
                                            in_=B_tr[:, 0:cn * P]),
                              waits=[mtr, (p.sems["ma"], p.sems["ma"].n),
                                     (p.sems["mg"], p.sems["mg"].n)]
                              if c0 == 0 else [mtr], mark=True)
                    p.use_bank("tr", mv)
                    mevs.append(mv)
                mev_sp = p.op("vector", lambda v: v.tensor_scalar(
                    out=q_h2[:, 0:160], in0=q_h2[:, 0:160], scalar1=1.0,
                    scalar2=None, op0=ALU.mult), mark=True)
                mrs_f = []
                for oi, o0 in enumerate((0, 512)):
                    pd = B_e if oi == 0 else B_f
                    bk = "de" if oi == 0 else "df"
                    mm = None
                    for c_ in range(NIC):
                        w0 = []
                        if c_ == 0:
                            w0 = [mev_sp, mwd, p.bank_wait(bk)]
                        mm = p.op("tensor", lambda tn, c_=c_, o0=o0, hqfm=hqfm,
                                  pd=pd: tn.matmul(
                            out=pd[:], lhsT=hqfm[:, c_ * P:(c_ + 1) * P],
                            rhs=wdq[c_][:, o0:o0 + 512],
                            start=(c_ == 0), stop=(c_ == NIC - 1)),
                            waits=w0, mark=(c_ == NIC - 1))
                    mr = p.op("vector", lambda v, o0=o0, b=x0b, pd=pd:
                              v.tensor_tensor(out=b[:, o0:o0 + 512],
                                              in0=b[:, o0:o0 + 512], in1=pd[:],
                                              op=ALU.add),
                              waits=[mm, mlx], mark=True)
                    p.use_bank(bk, mr)
                    mrs_f.append(mr)
                nfl = 6 if t == NT - 1 else 2
                for fi in range(nfl):
                    p.op("vector", lambda v: v.tensor_scalar(
                        out=q_h2[:, 0:512], in0=q_h2[:, 0:512], scalar1=1.0,
                        scalar2=None, op0=ALU.mult))
                p.dma("scalar", lambda e, t=t, b=x0b: e.dma_start(
                    out=x_dst[t * P:(t + 1) * P, :], in_=b[:]),
                    waits=mrs_f)

        def a_phase(pass_i):
            h0 = 2 * pass_i
            qoff = h0 * DK
            voff = h0 * DV
            wq = [wbf[:, c * 512:(c + 1) * 512] for c in range(NIF)]
            wk = [wbf[:, 4096 + c * 512:4096 + (c + 1) * 512] for c in range(NIF)]
            wv = [wbf[:, 8192 + c * 1024:8192 + (c + 1) * 1024] for c in range(NIF)]
            wg = [wbf[:, 16384 + c * 1024:16384 + (c + 1) * 1024]
                  for c in range(NIF)]
            wo = [wbf[:, 24576 + c * 1024:24576 + (c + 1) * 1024]
                  for c in range(NIF)]

            for ni, (name, wsrc, views, coff, width) in enumerate((
                    ("q", w_q, wq, qoff, 512), ("k", w_k, wk, qoff, 512))):
                pb = B_e if ni == 0 else B_f
                for c_ in range(NIF):
                    stg = wstage if c_ % 2 == 0 else wstage2
                    mld = p.dma("sync", lambda e, s_=stg, c_=c_, w=wsrc, co=coff:
                                e.dma_start(out=s_[:, 0:512],
                                            in_=w[c_ * P:(c_ + 1) * P,
                                                  co:co + 512]),
                                waits=[(p.sems["ma"], p.sems["ma"].n),
                                       (p.sems["mv"], p.sems["mv"].n),
                                       (p.sems["mp"], p.sems["mp"].n)])
                    mraw = p.op("vector", lambda v, s_=stg: v.tensor_copy(
                        out=wraw[:, 0:512], in_=s_[:, 0:512]),
                        waits=[mld, m_lnbbf,
                               (p.sems["mp"], p.sems["mp"].n)], mark=True)
                    w0 = [mraw]
                    if c_ == 0:
                        w0 = [mraw, p.bank_wait("pe" if ni == 0 else "pf")]
                    mbias = p.op("tensor", lambda tn, c_=c_, pb=pb: tn.matmul(
                        out=pb[0:1, 0:512],
                        lhsT=lnbbf_sb[:, c_:c_ + 1],
                        rhs=wraw[:, 0:512],
                        start=(c_ == 0), stop=(c_ == NIF - 1)),
                        waits=w0, mark=(c_ == NIF - 1))
                    p.op("vector", lambda v, s_=stg, c_=c_, vw=views:
                         v.tensor_scalar(out=vw[c_][:], in0=s_[:, 0:512],
                                         scalar1=lnw_sb[:, c_:c_ + 1],
                                         scalar2=None, op0=ALU.mult),
                         mark=True)
                mrow = p.op("vector", lambda v, ni=ni, pb=pb: v.tensor_copy(
                    out=bvrow[0:1, ni * 512:(ni + 1) * 512], in_=pb[0:1, 0:512]),
                    waits=[mbias], mark=True)
                p.use_bank("pe" if ni == 0 else "pf", mrow)
            m_qk_rows = mrow
            for ni, (name, wsrc, views) in enumerate((("v", w_v, wv),
                                                      ("g", w_g, wg))):
                for c_ in range(NIF):
                    stg = wstage if c_ % 2 == 0 else wstage2
                    mld = p.dma("sync", lambda e, s_=stg, c_=c_, w=wsrc:
                                e.dma_start(out=s_[:, 0:1024],
                                            in_=w[c_ * P:(c_ + 1) * P,
                                                  voff:voff + 1024]),
                                waits=[(p.sems["mv"], p.sems["mv"].n),
                                       (p.sems["mp"], p.sems["mp"].n)])
                    mraw = p.op("vector", lambda v, s_=stg: v.tensor_copy(
                        out=wraw[:, 0:1024], in_=s_[:, 0:1024]),
                        waits=[mld, (p.sems["mp"], p.sems["mp"].n)], mark=True)
                    for s2 in range(2):
                        pb = B_e if s2 == 0 else B_f
                        w0 = [mraw]
                        if c_ == 0:
                            w0 = [mraw, p.bank_wait("pe"), p.bank_wait("pf")]
                        mbias = p.op("tensor", lambda tn, s2=s2, c_=c_, pb=pb:
                                     tn.matmul(
                            out=pb[0:1, 0:512],
                            lhsT=lnbbf_sb[:, c_:c_ + 1],
                            rhs=wraw[:, s2 * 512:(s2 + 1) * 512],
                            start=(c_ == 0), stop=(c_ == NIF - 1)),
                            waits=w0, mark=(s2 == 1 and c_ == NIF - 1))
                    p.op("vector", lambda v, s_=stg, c_=c_, vw=views:
                         v.tensor_scalar(out=vw[c_][:], in0=s_[:, 0:1024],
                                         scalar1=lnw_sb[:, c_:c_ + 1],
                                         scalar2=None, op0=ALU.mult),
                         mark=True)
                mrow1 = p.op("vector", lambda v, ni=ni: v.tensor_copy(
                    out=bvrow[0:1, 1024 + ni * 1024:1024 + ni * 1024 + 512],
                    in_=B_e[0:1, 0:512]), waits=[mbias], mark=True)
                mrow = p.op("vector", lambda v, ni=ni: v.tensor_copy(
                    out=bvrow[0:1, 1024 + ni * 1024 + 512:1024 + ni * 1024 + 1024],
                    in_=B_f[0:1, 0:512]), waits=[mbias], mark=True)
                p.use_bank("pe", mrow1)
                p.use_bank("pf", mrow)
            m_vg_bias = mrow
            m_wo = None
            for c_ in range(NIF):
                stg = wstage if c_ % 2 == 0 else wstage2
                mld = p.dma("sync", lambda e, s_=stg, c_=c_: e.dma_start(
                    out=s_[:, 0:1024],
                    in_=w_o[voff + c_ * P:voff + (c_ + 1) * P, :]),
                    waits=[(p.sems["mv"], p.sems["mv"].n),
                           (p.sems["mp"], p.sems["mp"].n)])
                m_wo = p.op("vector", lambda v, s_=stg, c_=c_: v.tensor_copy(
                    out=wo[c_][:], in_=s_[:, 0:1024]), waits=[mld], mark=True)

            if pass_i == 0:
                p.dbg("a_bvrow", bvrow[:], [m_vg_bias])
                p.dbg("a_wo0", wo[0][:], [m_wo])
                p.dbg("a_wq0", wq[0][:], [(p.sems["mv"], p.sems["mv"].n)])
            if pass_i == 0:
                emit_stats_pass(x1_s, with_R=True)
            m_stats = (p.sems["mv"], p.sems["mv"].n)

            mS0 = p.op("vector", lambda v: v.memset(S_f[:], 0.0), mark=True)
            mSb0 = p.op("gpsimd", lambda g: g.memset(S_b[:], 0.0),
                        waits=[mS0], mark=True)

            m_x2dma = None
            for cidx in range(NCH):
                t0 = cidx * CH
                x1b = xa if cidx % 2 == 0 else xb
                mlx1 = p.dma("sync", lambda e, b=x1b, t0=t0: e.dma_start(
                    out=b[:], in_=x1_s[t0:t0 + CH, :]),
                    waits=[(p.sems["mv"], p.sems["mv"].n),
                           (p.sems["qa"], p.sems["qa"].n)])
                if pass_i == 1:
                    mlx2 = p.dma("sync", lambda e, t0=t0: e.dma_start(
                        out=x2pc[:], in_=x2p_s[t0:t0 + CH, :]),
                        waits=[(p.sems["mv"], p.sems["mv"].n),
                               (p.sems["qa"], p.sems["qa"].n)])
                co0 = (cidx % 2) * 2 * CH
                mcs = p.dma("sync", lambda e, t0=t0, co0=co0: e.dma_start(
                    out=cs_sb[:, co0:co0 + CH], in_=cosf[:, t0:t0 + CH]),
                    waits=[(p.sems["mv"], p.sems["mv"].n)])
                mcs = p.dma("sync", lambda e, t0=t0, co0=co0: e.dma_start(
                    out=cs_sb[:, co0 + CH:co0 + 2 * CH], in_=sinf[:, t0:t0 + CH]))
                mn2 = None
                m_nbf = None
                if pass_i == 0:
                    mn2 = p.op("vector", lambda v, b=x1b, t=cidx: v.tensor_scalar(
                        out=n2a[:], in0=b[:], scalar1=st_nmu[:, t:t + 1],
                        scalar2=st_rstd1[:, t:t + 1], op0=ALU.add, op1=ALU.mult),
                        waits=[mlx1, m_stats], mark=True)
                    m_nbf = p.op("vector", lambda v: v.tensor_copy(
                        out=nbf[:, 0:D], in_=n2a[:]),
                        waits=[(p.sems["mp"], p.sems["mp"].n),
                               (p.sems["qg"], p.sems["qg"].n)], mark=True)
                mnrm = p.op("vector", lambda v, b=x1b, t=cidx: v.tensor_scalar(
                    out=nrm_bf[:], in0=b[:], scalar1=st_nmu[:, t:t + 1],
                    scalar2=st_R[:, t:t + 1],
                    op0=ALU.add, op1=ALU.mult),
                    waits=[mlx1, m_stats,
                           (p.sems["mp"], p.sems["mp"].n)], mark=True)
                if cidx == 0:
                    if pass_i == 0:
                        p.dbg("a_n2", n2a[:], [mn2])
                    p.dbg(f"a{pass_i}_nrm", nrm_bf, [mnrm])
                mnrm = p.op("vector", lambda v: v.tensor_scalar(
                    out=q_u[:, 0:160], in0=q_u[:, 0:160], scalar1=1.0,
                    scalar2=None, op0=ALU.mult), mark=True)
                mtr = None
                for c_ in range(NIF):
                    mtr = p.op("tensor", lambda tn, c_=c_: tn.transpose(
                        out=B_tr[:, c_ * P:(c_ + 1) * P],
                        in_=nrm_bf[:, c_ * P:(c_ + 1) * P], identity=id_sb[:]),
                        waits=[mnrm, p.bank_wait("tr")] if c_ == 0 else [],
                        mark=(c_ == NIF - 1))
                mfm = p.op("vector", lambda v: v.tensor_copy(
                    out=zq_fm[:, 0:1024], in_=B_tr[:, 0:1024]),
                    waits=[mtr], mark=True)
                p.use_bank("tr", mfm)
                mfm = p.op("vector", lambda v: v.tensor_scalar(
                    out=q_u[:, 160:320], in0=q_u[:, 160:320], scalar1=1.0,
                    scalar2=None, op0=ALU.mult), mark=True)
                nrm_fm_v = [zq_fm[:, c * P:(c + 1) * P] for c in range(NIF)]

                for pj, (wv_, pq, bk) in enumerate(((wq, B_a, "pa"),
                                                    (wk, B_b, "pb"))):
                    boff = pj * 512
                    for m in range(4):
                        for c_ in range(NIF):
                            w0 = []
                            if m == 0 and c_ == 0:
                                w0 = [mfm, m_qk_rows, p.bank_wait(bk)]
                            p.op("tensor", lambda tn, m=m, c_=c_, wv_=wv_, pq=pq:
                                 tn.matmul(
                                     out=pq[:, m * P:(m + 1) * P],
                                     lhsT=wv_[c_][:, m * P:(m + 1) * P],
                                     rhs=nrm_fm_v[c_],
                                     start=(c_ == 0), stop=False),
                                 waits=w0)
                        p.op("tensor", lambda tn, m=m, pq=pq, boff=boff:
                             tn.matmul(
                                 out=pq[:, m * P:(m + 1) * P],
                                 lhsT=bvrow[0:1, boff + m * P:boff + (m + 1) * P],
                                 rhs=ones_sb[:],
                                 start=False, stop=True),
                             mark=(m == 3))
                mpk = (p.sems["mp"], p.sems["mp"].n)
                if cidx == 0 and p.dbg_on:
                    mqp = p.op("vector", lambda v: v.tensor_copy(
                        out=hf[:, 0:512], in_=B_a[:]), waits=[mpk], mark=True)
                    p.dbg(f"a{pass_i}_qpsum", hf[:, 0:512], [mqp])
                    mqp = p.op("vector", lambda v: v.tensor_copy(
                        out=hf[:, 512:1024], in_=B_b[:]), waits=[mpk], mark=True)
                    p.dbg(f"a{pass_i}_kpsum", hf[:, 512:1024], [mqp])
                csl = cs_sb[:, co0:co0 + CH]
                ssl = cs_sb[:, co0 + CH:co0 + 2 * CH]
                m_last = None
                for gi, (dst, pq, boff) in enumerate(((qbf, B_a, 0),
                                                      (kbf, B_b, 4))):
                    slot = 0
                    m_prod = None
                    for h2 in range(2):
                        ce = (2 * h2) * P
                        co = (2 * h2 + 1) * P
                        for (src_, tbl) in ((ce, csl), (co, ssl),
                                            (co, csl), (ce, ssl)):
                            sl = r_tt[:, slot * P:(slot + 1) * P]
                            w0 = ()
                            if slot == 0:
                                w0 = [mpk, mcs]
                            m_prod = p.op("vector", lambda v, pq=pq, src_=src_,
                                          tbl=tbl, sl=sl: v.tensor_tensor(
                                out=sl, in0=pq[:, src_:src_ + P],
                                in1=tbl, op=ALU.mult),
                                waits=w0, mark=(slot == 7))
                            slot += 1
                    for h2 in range(2):
                        ce = (2 * h2) * P
                        co = (2 * h2 + 1) * P
                        s0 = 4 * h2
                        p.op("vector", lambda v, dst=dst, ce=ce, s0=s0:
                             v.tensor_tensor(
                                 out=dst[:, ce:ce + P],
                                 in0=r_tt[:, s0 * P:(s0 + 1) * P],
                                 in1=r_tt[:, (s0 + 1) * P:(s0 + 2) * P],
                                 op=ALU.subtract),
                             waits=[m_prod] if h2 == 0 else ())
                        m_last = p.op("vector", lambda v, dst=dst, co=co, s0=s0:
                                      v.tensor_tensor(
                                          out=dst[:, co:co + P],
                                          in0=r_tt[:, (s0 + 2) * P:(s0 + 3) * P],
                                          in1=r_tt[:, (s0 + 3) * P:(s0 + 4) * P],
                                          op=ALU.add),
                                      mark=(gi == 1 and h2 == 1))
                m_rope = m_last
                m_rope = p.op("vector", lambda v: v.tensor_scalar(
                    out=q_u[:, 0:256], in0=q_u[:, 0:256], scalar1=1.0,
                    scalar2=None, op0=ALU.mult), mark=True)
                p.use_bank("pa", m_rope)
                p.use_bank("pb", m_rope)
                if cidx == 0:
                    p.dbg(f"a{pass_i}_q", qbf, [m_rope])
                    p.dbg(f"a{pass_i}_k", kbf, [m_rope])
                mtr = None
                for j in range(4):
                    mtr = p.op("tensor", lambda tn, j=j: tn.transpose(
                        out=B_tr[:, j * P:(j + 1) * P],
                        in_=kbf[:, j * P:(j + 1) * P], identity=id_sb[:]),
                        waits=[m_rope, p.bank_wait("tr")] if j == 0 else [],
                        mark=(j == 3))
                mktm = p.op("vector", lambda v: v.tensor_copy(
                    out=ktm[:], in_=B_tr[:, 0:512]), waits=[mtr], mark=True)
                p.use_bank("tr", mktm)

                for pj, (wv_, broff) in enumerate(((wv, 1024), (wg, 2048))):
                    for s2 in range(2):
                        pq = B_c if s2 == 0 else B_d
                        bk = "pc" if s2 == 0 else "pd"
                        for c_ in range(NIF):
                            w0 = []
                            if c_ == 0:
                                w0 = [mfm, m_vg_bias, p.bank_wait(bk)]
                            p.op("tensor", lambda tn, c_=c_, s2=s2, wv_=wv_, pq=pq:
                                 tn.matmul(
                                     out=pq[:],
                                     lhsT=nrm_fm_v[c_],
                                     rhs=wv_[c_][:, s2 * 512:(s2 + 1) * 512],
                                     start=(c_ == 0), stop=False),
                                 waits=w0)
                        p.op("tensor", lambda tn, s2=s2, br=broff, pq=pq:
                             tn.matmul(out=pq[:], lhsT=ones_sb[:],
                                       rhs=bvrow[0:1, br + s2 * 512:
                                                 br + (s2 + 1) * 512],
                                       start=False, stop=True),
                             mark=(s2 == 1))
                    mpvg = (p.sems["mp"], p.sems["mp"].n)
                    if pj == 0:
                        mv1 = p.op("scalar", lambda s: s.copy(
                            out=vsb[:, 0:512], in_=B_c[:]),
                            waits=[mpvg, (p.sems["mg"], p.sems["mg"].n)],
                            mark=True)
                        mv2 = p.op("scalar", lambda s: s.copy(
                            out=vsb[:, 512:1024], in_=B_d[:]), mark=True)
                        p.use_bank("pc", mv1)
                        p.use_bank("pd", mv2)
                        m_v = mv2
                    else:
                        ms1 = p.op("scalar", lambda s: s.activation(
                            out=tgt[:, 0:512], in_=B_c[:], func=AF.Silu),
                            waits=[mpvg, (p.sems["mv"], p.sems["mv"].n)],
                            mark=True)
                        ms2 = p.op("scalar", lambda s: s.activation(
                            out=tgt[:, 512:1024], in_=B_d[:], func=AF.Silu),
                            mark=True)
                        p.use_bank("pc", ms1)
                        p.use_bank("pd", ms2)
                        gv = tgt[:].rearrange("p (h d) -> p h d", h=2)
                        gb = gnw_sb[:].rearrange("p (o d) -> p o d",
                                                 o=1).broadcast_to([P, 2, DV])
                        m_tgt = p.op("vector", lambda v: v.tensor_tensor(
                            out=gv, in0=gv, in1=gb, op=ALU.mult),
                            waits=[ms1, ms2], mark=True)

                if cidx == 0:
                    p.dbg(f"a{pass_i}_ktm", ktm, [mktm])
                    p.dbg(f"a{pass_i}_v", vsb, [m_v])
                    p.dbg(f"a{pass_i}_tgt", tgt, [m_tgt])
                m_sq = []
                m_ofin_dep = []
                for h2 in range(2):
                    gh = h0 + h2
                    msc = None
                    for j in range(2):
                        msc = p.op("tensor", lambda tn, j=j, h2=h2: tn.matmul(
                            out=B_e[:, 0:128],
                            lhsT=kbf[:, (2 * h2 + j) * P:(2 * h2 + j + 1) * P],
                            rhs=qbf[:, (2 * h2 + j) * P:(2 * h2 + j + 1) * P],
                            start=(j == 0), stop=(j == 1)),
                            waits=[m_rope, p.bank_wait("pe")] if j == 0 else [],
                            mark=(j == 1))
                    mscm = p.op("vector", lambda v, gh=gh: v.tensor_tensor(
                        out=scm[:], in0=B_e[:, 0:128],
                        in1=dm_sb[:, gh * CH:(gh + 1) * CH], op=ALU.mult),
                        waits=[msc, cl["dm"]], mark=True)
                    p.use_bank("pe", mscm)
                    mscm = p.op("vector", lambda v: v.tensor_scalar(
                        out=q_u[:, 256:512], in0=q_u[:, 256:512], scalar1=1.0,
                        scalar2=None, op0=ALU.mult), mark=True)
                    if cidx == 0 and h2 == 0:
                        p.dbg(f"a{pass_i}_scm", scm, [mscm])
                    mqc1 = p.op("vector", lambda v, h2=h2, gh=gh: v.tensor_tensor(
                        out=qc1[:], in0=qbf[:, (2 * h2) * P:(2 * h2 + 1) * P],
                        in1=cr_sb[:, gh * CH:(gh + 1) * CH], op=ALU.mult),
                        waits=[m_rope, cl["cr"],
                               (p.sems["mp"], p.sems["mp"].n)], mark=True)
                    mqc2 = p.op("vector", lambda v, h2=h2, gh=gh: v.tensor_tensor(
                        out=qc2[:], in0=qbf[:, (2 * h2 + 1) * P:(2 * h2 + 2) * P],
                        in1=cr_sb[:, gh * CH:(gh + 1) * CH], op=ALU.mult),
                        mark=True)
                    mqc2 = p.op("vector", lambda v: v.tensor_scalar(
                        out=q_u[:, 512:768], in0=q_u[:, 512:768], scalar1=1.0,
                        scalar2=None, op0=ALU.mult), mark=True)
                    po = B_f if h2 == 0 else B_g
                    bo = "pf" if h2 == 0 else "pg"
                    p.op("tensor", lambda tn, h2=h2, po=po: tn.matmul(
                        out=po[:], lhsT=scm[:], rhs=vsb[:, h2 * 512:(h2 + 1) * 512],
                        start=True, stop=False),
                        waits=[mscm, m_v, p.bank_wait(bo)])
                    p.op("tensor", lambda tn, h2=h2, po=po: tn.matmul(
                        out=po[:], lhsT=qc1[:], rhs=S_b[:, (2 * h2) * DV:
                                                        (2 * h2 + 1) * DV],
                        start=False, stop=False), waits=[mqc1, mSb0])
                    mo = p.op("tensor", lambda tn, h2=h2, po=po: tn.matmul(
                        out=po[:], lhsT=qc2[:], rhs=S_b[:, (2 * h2 + 1) * DV:
                                                        (2 * h2 + 2) * DV],
                        start=False, stop=True), waits=[mqc2], mark=True)
                    mvn = p.op("gpsimd", lambda g, h2=h2, gh=gh: g.tensor_scalar(
                        out=vnp[:], in0=vsb[:, h2 * 512:(h2 + 1) * 512],
                        scalar1=kd_sb[:, gh:gh + 1], scalar2=None, op0=ALU.mult),
                        waits=[m_v, cl["kd"],
                               (p.sems["mp"], p.sems["mp"].n)], mark=True)
                    for m in range(2):
                        pu_ = B_a if m == 0 else B_b
                        bu_ = "pa" if m == 0 else "pb"
                        mU = p.op("tensor", lambda tn, h2=h2, m=m, pu_=pu_:
                                  tn.matmul(
                            out=pu_[:], lhsT=ktm[:, (2 * h2 + m) * P:
                                                 (2 * h2 + m + 1) * P],
                            rhs=vnp[:], start=True, stop=True),
                            waits=[mktm, mvn, p.bank_wait(bu_)], mark=True)
                        mSf = p.op("vector", lambda v, h2=h2, m=m, pu_=pu_, gh=gh:
                                   v.scalar_tensor_tensor(
                            out=S_f[:, (2 * h2 + m) * DV:(2 * h2 + m + 1) * DV],
                            in0=S_f[:, (2 * h2 + m) * DV:(2 * h2 + m + 1) * DV],
                            scalar=GAMMA_C[gh], in1=pu_[:],
                            op0=ALU.mult, op1=ALU.add),
                            waits=[mU, mSb0], mark=True)
                        p.use_bank(bu_, mSf)
                        mSb0 = p.op("vector", lambda v, h2=h2, m=m: v.tensor_copy(
                            out=S_b[:, (2 * h2 + m) * DV:(2 * h2 + m + 1) * DV],
                            in_=S_f[:, (2 * h2 + m) * DV:(2 * h2 + m + 1) * DV]),
                            waits=[mo], mark=True)
                    m_sqa = p.op("scalar", lambda s, h2=h2, po=po:
                                 s.activation(out=sqsc[:, 0:DV], in_=po[:],
                                              func=AF.Square),
                                 waits=[mo, (p.sems["mv"], p.sems["mv"].n)],
                                 mark=True)
                    m_sq.append(p.op("vector", lambda v, h2=h2: v.reduce_sum(
                        out=osum[:, h2:h2 + 1], in_=sqsc[:, 0:DV], axis=AX.X),
                        waits=[m_sqa], mark=True))
                    m_ofin_dep.append((mo, po, bo, h2))
                if cidx == 0:
                    p.dbg(f"a{pass_i}_Sf", S_f[:], [(p.sems["mv"], p.sems["mv"].n)])
                    p.dbg(f"a{pass_i}_osum", osum[:], m_sq)
                p.op("vector", lambda v: v.tensor_scalar(
                    out=q_u[:, 0:256], in0=q_u[:, 0:256], scalar1=1.0,
                    scalar2=None, op0=ALU.mult), waits=m_sq)
                p.op("vector", lambda v: v.tensor_scalar(
                    out=q_u[:, 256:512], in0=q_u[:, 256:512], scalar1=1.0,
                    scalar2=None, op0=ALU.mult))
                p.op("vector", lambda v: v.tensor_scalar(
                    out=ot1[:], in0=osum[:], scalar1=1.0 / DV, scalar2=EPS,
                    op0=ALU.mult, op1=ALU.add))
                emit_rsqrt(orst[:], ot1[:], ot2[:], ot3[:], ot4[:])
                p.op("vector", lambda v: v.tensor_scalar(
                    out=q_u[:, 0:256], in0=q_u[:, 0:256], scalar1=1.0,
                    scalar2=None, op0=ALU.mult))
                p.op("vector", lambda v: v.tensor_scalar(
                    out=q_u[:, 256:512], in0=q_u[:, 256:512], scalar1=1.0,
                    scalar2=None, op0=ALU.mult))
                if p.dbg_on:
                    mosd = p.op("vector", lambda v, t=cidx: v.tensor_copy(
                        out=hf[:, 1300:1302], in_=osum[:, 0:2]), mark=True)
                    mosd = p.op("vector", lambda v, t=cidx: v.tensor_copy(
                        out=hf[:, 1302:1304], in_=orst[:, 0:2]), mark=True)
                    p.dma("gpsimd", lambda e, t=pass_i * NCH + cidx: e.dma_start(
                        out=dbg_os_all[t], in_=hf[:, 1300:1304]), waits=[mosd])
                for mo, po, bo, h2 in m_ofin_dep:
                    ob = orst[:, h2:h2 + 1].broadcast_to([P, 512])
                    mof = p.op("vector", lambda v, po=po, h2=h2:
                               v.tensor_tensor(
                        out=ofin[:, h2 * 512:(h2 + 1) * 512], in0=po[:],
                        in1=tgt[:, h2 * 512:(h2 + 1) * 512], op=ALU.mult),
                        waits=[m_tgt, (p.sems["qg"], p.sems["qg"].n),
                               (p.sems["mp"], p.sems["mp"].n)], mark=True)
                    mof = p.op("vector", lambda v, h2=h2, ob=ob: v.tensor_tensor(
                        out=ofin[:, h2 * 512:(h2 + 1) * 512],
                        in0=ofin[:, h2 * 512:(h2 + 1) * 512],
                        in1=ob, op=ALU.mult), mark=True)
                    p.use_bank(bo, mof)
                m_ofin = mof
                m_ofin = p.op("vector", lambda v: v.tensor_scalar(
                    out=q_u[:, 480:640], in0=q_u[:, 480:640], scalar1=1.0,
                    scalar2=None, op0=ALU.mult), mark=True)
                if cidx == 0:
                    p.dbg(f"a{pass_i}_ofin", ofin, [m_ofin])
                if pass_i == 0 and p.dbg_on:
                    p.dma("gpsimd", lambda e, t0=t0: e.dma_start(
                        out=dbg_ofin_all[t0:t0 + CH, :], in_=ofin),
                        waits=[m_ofin])
                mtr = None
                for c_ in range(NIF):
                    mtr = p.op("tensor", lambda tn, c_=c_: tn.transpose(
                        out=B_tr[:, c_ * P:(c_ + 1) * P],
                        in_=ofin[:, c_ * P:(c_ + 1) * P], identity=id_sb[:]),
                        waits=[m_ofin, p.bank_wait("tr")] if c_ == 0 else [],
                        mark=(c_ == NIF - 1))
                mofm = p.op("vector", lambda v: v.tensor_copy(
                    out=ofm[:], in_=B_tr[:, 0:1024]),
                    waits=[mtr, (p.sems["qg"], p.sems["qg"].n)], mark=True,
                    edge=True)
                mofm_d = mofm
                mofm = p.op("vector", lambda v: v.tensor_scalar(
                    out=q_u[:, 640:800], in0=q_u[:, 640:800], scalar1=1.0,
                    scalar2=None, op0=ALU.mult), mark=True)
                if pass_i == 0 and p.dbg_on:
                    p.dma("gpsimd", lambda e, t0=t0: e.dma_start(
                        out=dbg_ofm_all[t0:t0 + CH, :], in_=ofm), waits=[mofm])
                p.use_bank("tr", mofm)
                if cidx == 0:
                    p.dbg(f"a{pass_i}_ofm", ofm, [mofm])
                mrs = []
                for oi in range(2):
                    pw = B_c if oi == 0 else B_d
                    bk = "pc" if oi == 0 else "pd"
                    mm = None
                    if pass_i == 0:
                        p.op("tensor", lambda tn, oi=oi, pw=pw: tn.matmul(
                            out=pw[:], lhsT=id_sb[:],
                            rhs=nbf[:, oi * 512:(oi + 1) * 512],
                            start=True, stop=False),
                            waits=[m_nbf, p.bank_wait(bk)])
                    for c_ in range(NIF):
                        w0 = []
                        if c_ == 0:
                            w0 = [mofm, m_wo]
                            if pass_i != 0:
                                w0.append(p.bank_wait(bk))
                        mm = p.op("tensor", lambda tn, c_=c_, oi=oi, pw=pw,
                                  pi=pass_i: tn.matmul(
                            out=pw[:], lhsT=ofm[:, c_ * P:(c_ + 1) * P],
                            rhs=wo[c_][:, oi * 512:(oi + 1) * 512],
                            start=(c_ == 0 and pi != 0), stop=(c_ == NIF - 1)),
                            waits=w0, mark=(c_ == NIF - 1))
                    if pass_i == 0:
                        mr_ = p.op("vector", lambda v, oi=oi, pw=pw, b=x1b:
                                   v.tensor_tensor(
                            out=b[:, oi * 512:(oi + 1) * 512],
                            in0=b[:, oi * 512:(oi + 1) * 512], in1=pw[:],
                            op=ALU.add), waits=[mm], mark=True)
                    else:
                        mr_ = p.op("vector", lambda v, oi=oi, pw=pw:
                                   v.tensor_tensor(
                            out=x2pc[:, oi * 512:(oi + 1) * 512],
                            in0=x2pc[:, oi * 512:(oi + 1) * 512], in1=pw[:],
                            op=ALU.add), waits=[mm, mlx2], mark=True)
                    p.use_bank(bk, mr_)
                    mrs.append(mr_)
                nfl = 6 if cidx == NCH - 1 else 2
                for fi in range(nfl):
                    p.op("vector", lambda v: v.tensor_scalar(
                        out=q_u[:, 0:512], in0=q_u[:, 0:512], scalar1=1.0,
                        scalar2=None, op0=ALU.mult))
                if pass_i == 0:
                    m_x2dma = p.dma("scalar", lambda e, t0=t0, b=x1b: e.dma_start(
                        out=x2p_s[t0:t0 + CH, :], in_=b[:]), waits=mrs)
                else:
                    m_x2dma = p.dma("scalar", lambda e, t0=t0: e.dma_start(
                        out=x2_s[t0:t0 + CH, :], in_=x2pc[:]), waits=mrs)


        f_phase(x_in, x1_s, first=True)
        p.barrier()
        a_phase(0)
        p.barrier()
        a_phase(1)
        p.barrier()
        f_phase(x2_s, out_t, first=False)
        p.ops["sync"].append((None, [(p.sems["qa"], p.sems["qa"].n),
                                     (p.sems["qs"], p.sems["qs"].n)], []))

        block = es.enter_context(nc.Block())
        p.build(block)
    return nc


_NC_CACHE = {}


def _get_nc():
    if "nc" not in _NC_CACHE:
        _NC_CACHE["nc"] = build_program(dev_dump=False)
    return _NC_CACHE["nc"]


def kernel(L0_attn_x, L0_attn_ln_w, L0_attn_ln_b, L0_attn_Wq, L0_attn_Wk,
           L0_attn_Wv, L0_attn_Wg, L0_attn_g_norm_w, L0_attn_Wo,
           L0_ffn_w_gate, L0_ffn_w_up, L0_ffn_w_down):
    x = np.ascontiguousarray(np.asarray(L0_attn_x, np.float32))
    const = host_constants()
    lnw = np.asarray(L0_attn_ln_w, np.float32)
    lnb = np.asarray(L0_attn_ln_b, np.float32)
    base = {
        "w_q": np.ascontiguousarray(np.asarray(L0_attn_Wq, np.float32)),
        "w_k": np.ascontiguousarray(np.asarray(L0_attn_Wk, np.float32)),
        "w_v": np.ascontiguousarray(np.asarray(L0_attn_Wv, np.float32)),
        "w_g": np.ascontiguousarray(np.asarray(L0_attn_Wg, np.float32)),
        "w_o": np.ascontiguousarray(np.asarray(L0_attn_Wo, np.float32)),
        "w_gate": np.ascontiguousarray(np.asarray(L0_ffn_w_gate, np.float32)),
        "w_up": np.ascontiguousarray(np.asarray(L0_ffn_w_up, np.float32)),
        "w_down": np.ascontiguousarray(np.asarray(L0_ffn_w_down, np.float32)),
        "lnw_c": np.ascontiguousarray(lnw.reshape(NIF, P).T),
        "lnb_c": np.ascontiguousarray(lnb.reshape(NIF, P).T),
        "gnw_b": np.ascontiguousarray(np.broadcast_to(
            np.asarray(L0_attn_g_norm_w, np.float32)[None, :], (P, DV))),
        "dmaskT": const["dmaskT"], "crossb": const["crossb"],
        "kdec": const["kdec"], "cosf": const["cosf"], "sinf": const["sinf"],
        "ident": const["ident"], "ones1": const["ones1"],
    }
    in_maps = [dict(base, x0=np.ascontiguousarray(x[i])) for i in range(8)]
    nc = _get_nc()
    res = run_bass_kernel_spmd(nc, in_maps, core_ids=list(range(8)))
    return np.stack([np.asarray(r["out"], np.float32) for r in res.results])
